# version 01
# baseline (speedup 1.0000x reference)
import numpy as np
import concourse.bass as bass
import concourse.mybir as mybir
from contextlib import ExitStack

F32 = mybir.dt.float32
BF16 = mybir.dt.bfloat16
AF = mybir.ActivationFunctionType
ALU = mybir.AluOpType

D = 4096
NT = 32
L = 4096
NH = 16
HALF = 2048
EPS = 1e-5


class Ctr:
    def __init__(self, ctx, nc, name):
        self.h = ctx.enter_context(nc.semaphore(name))
        self.n = 0

    def inc(self, instr, k=1):
        instr.then_inc(self.h, k)
        self.n += k
        return instr


class MK:
    def __init__(self, dbg_modes=None):
        self.dbg = dbg_modes or {}
        self.nc = bass.Bass("TRN2", target_bir_lowering=False)
        self.ctx = ExitStack()
        self.pctx = None
        self.res = {}

    def mark(self, name, ctr, lag=0, add=False):
        if add and name in self.res:
            self.res[name][id(ctr)] = (ctr, ctr.n - lag)
        else:
            self.res[name] = {id(ctr): (ctr, ctr.n - lag)}

    def guard(self, eng, name):
        for ctr, n in self.res.get(name, {}).values():
            if n > 0:
                eng.wait_ge(ctr.h, n)

    def dram(self, name, shape, dtype, kind="Internal"):
        kind = {"in": "ExternalInput", "out": "ExternalOutput"}.get(self.dbg.get(name), kind)
        return self.nc.dram_tensor(name, shape, dtype, kind=kind).ap()

    def sb(self, name, shape, dtype):
        return self.ctx.enter_context(self.nc.sbuf_tensor(name, shape, dtype))

    def psb(self, name, shape, dtype):
        return self.pctx.enter_context(self.nc.sbuf_tensor(name, shape, dtype))

    def phase_open(self):
        assert self.pctx is None
        self.pctx = ExitStack()

    def phase_close(self, block):
        self.barrier(block)
        self.pctx.close()
        self.pctx = None

    def ps(self, name, shape):
        return self.ctx.enter_context(self.nc.psum_tensor(name, shape, F32))

    def declare(self):
        dram, sb = self.dram, self.sb
        self.x_img = dram("x_img", [128, NT, L], F32, kind="ExternalInput")
        self.x_own = dram("x_own", [128, NT, HALF], F32, kind="ExternalInput")
        self.wq_img = dram("wq_img", [NH, 128, NT * 128], F32, kind="ExternalInput")
        self.wk_img = dram("wk_img", [NH, 128, NT * 128], F32, kind="ExternalInput")
        self.wv_img = dram("wv_img", [8, 128, NT * 256], F32, kind="ExternalInput")
        self.wo_img = dram("wo_img", [32, 128, NT * 128], F32, kind="ExternalInput")
        self.wf_img = dram("wf_img", [32, 128, NT * 128], F32, kind="ExternalInput")
        self.bq_i = dram("bq_i", [128, NH], F32, kind="ExternalInput")
        self.bk_i = dram("bk_i", [128, NH], F32, kind="ExternalInput")
        self.bv_i = dram("bv_i", [1, NH * 128], F32, kind="ExternalInput")
        self.bo_i = dram("bo_i", [128, 32], F32, kind="ExternalInput")
        self.bf_i = dram("bf_i", [128, 32], F32, kind="ExternalInput")
        self.gw_img = dram("gw_img", [128, NT, 8], F32, kind="ExternalInput")
        self.ics_i = dram("ics_i", [8, D], F32, kind="ExternalInput")
        self.ocs_i = dram("ocs_i", [8, D], F32, kind="ExternalInput")
        self.mask_i = dram("mask_i", [128, 4, 512], F32, kind="ExternalInput")
        self.sel_i = dram("sel_i", [128, 2], F32, kind="ExternalInput")
        self.out_img = dram("out_img", [128, NT, HALF], F32, kind="ExternalOutput")
        self.ln1_s = dram("ln1_s", [128, NT, L], BF16)
        self.qt_s = dram("qt_s", [NH, 128, L], BF16)
        self.kt_s = dram("kt_s", [NH, 128, L], BF16)
        self.v_s = dram("v_s", [NH, 32, 128, 128], BF16)
        self.ao_cc = dram("ao_cc", [NH, 128, L], BF16)
        self.ao_g = [dram(f"ao_g{k}", [8, 128, L], BF16) for k in range(4)]
        self.y_s = dram("y_s", [128, NT, HALF], BF16)
        self.x2_s = dram("x2_s", [128, NT, HALF], BF16)
        self.fo_s = dram("fo_s", [128, NT, HALF], BF16)
        self.xs_s = dram("xs_s", [128, NT, HALF], BF16)
        self.p1_dbg = dram("p1_dbg", [8, 4, 512], F32)
        self.p3_dbg = dram("p3_dbg", [4, 128, 512], F32)
        self.bq_sb = sb("bq_sb", [128, NH], F32)
        self.bk_sb = sb("bk_sb", [128, NH], F32)
        self.bv_sb = sb("bv_sb", [1, NH * 128], BF16)
        self.bo_sb = sb("bo_sb", [128, 32], F32)
        self.bf_sb = sb("bf_sb", [128, 32], F32)
        self.gw_sb = sb("gw_sb", [128, NT * 8], BF16)
        self.sel_sb = sb("sel_sb", [128, 2], F32)
        self.eps_sb = sb("eps_sb", [1, 1], F32)
        self.ones1f = sb("ones1f", [1, 128], F32)
        self.ones1b = sb("ones1b", [1, 128], BF16)
        self.o128b = sb("o128b", [128, 1], BF16)
        self.o128f = sb("o128f", [128, 1], F32)
        self.scrap1 = sb("scrap1", [1, 8], F32)
        self.scrap2 = sb("scrap2", [1, 8], F32)
        self.scrap3 = sb("scrap3", [1, 8], F32)
        self.rt_all = sb("rt_all", [8, 4 * 512], BF16)
        self.p_mm = self.ps("p_mm", [128, 2048])
        self.p_bc = self.ps("p_bc", [128, 512])
        self.p_bc2 = self.ps("p_bc2", [128, 512])
        self.p_st1 = self.ps("p_st1", [1, 512])
        self.p_st2 = self.ps("p_st2", [1, 512])
        self.gidx = 0
        ctx, nc = self.ctx, self.nc
        self.sG = Ctr(ctx, nc, "sG")
        self.sS = Ctr(ctx, nc, "sS")
        self.sMM = Ctr(ctx, nc, "sMM")
        self.sV = Ctr(ctx, nc, "sV")
        self.sA = Ctr(ctx, nc, "sA")
        self.sCC = Ctr(ctx, nc, "sCC")
        self.bar = Ctr(ctx, nc, "bar")

    def mm_bank(self, t):
        b = self.gidx % 4
        self.gidx += 1
        self.guard(t, f"bk{b}")
        return b

    def barrier(self, block):
        def _(v):
            self.sV.inc(v.tensor_copy(self.scrap2[:], self.scrap1[:]))
        block.vector(_)
        def _(s):
            self.sA.inc(s.copy(self.scrap3[:], self.scrap1[:]))
        block.scalar(_)
        waits = [(self.sMM, self.sMM.n), (self.sV, self.sV.n), (self.sA, self.sA.n),
                 (self.sS, self.sS.n), (self.sG, self.sG.n), (self.sCC, self.sCC.n)]
        for deco in (block.tensor, block.vector, block.scalar, block.gpsimd, block.sync):
            def _(eng):
                for ctr, n in waits:
                    if n > 0:
                        eng.wait_ge(ctr.h, n)
            deco(_)

    def p0_consts(self, block):
        def _(g):
            for dst, src in ((self.bq_sb[:], self.bq_i), (self.bk_sb[:], self.bk_i),
                             (self.bv_sb[:], self.bv_i), (self.bo_sb[:], self.bo_i),
                             (self.bf_sb[:], self.bf_i),
                             (self.gw_sb[:].rearrange("p (t e) -> p t e", t=NT), self.gw_img),
                             (self.sel_sb[:], self.sel_i)):
                self.sG.inc(g.dma_start(out=dst, in_=src), 16)
        block.gpsimd(_)
        def _(v):
            v.memset(self.eps_sb[:], EPS)
            v.memset(self.ones1f[:], 1.0)
            v.memset(self.ones1b[:], 1.0)
            v.memset(self.o128b[:], 1.0)
            v.memset(self.scrap1[:], 0.0)
            self.sV.inc(v.memset(self.o128f[:], 1.0))
        block.vector(_)
        def _(s):
            from concourse.hw_specs import get_activation_tables
            for idx, (nm, funcs) in enumerate(get_activation_tables(self.nc.m.arch).items()):
                if nm == "natural_log_exp_and_others":
                    s.add_instruction(mybir.InstLoadActFuncSet(
                        name=self.nc.get_next_instruction_name(),
                        act_func_set_id=idx, ins=[], outs=[]))
                    break
        block.scalar(_)
        self.barrier(block)

    def ln_stats(self, block, src_sb, w, waits=()):
        sV, sA, sMM = self.sV, self.sA, self.sMM
        def _(v):
            for ctr, n in waits:
                v.wait_ge(ctr.h, n)
            self.guard(v, "p1sq")
            self.sV.inc(v.tensor_tensor(w.sq[:], src_sb[:], src_sb[:], ALU.mult))
        block.vector(_)
        v_sq = sV.n
        def _(t):
            for ctr, n in waits:
                t.wait_ge(ctr.h, n)
            self.guard(t, "p_st")
            for dt in range(NT):
                t.matmul(self.p_st1[:], self.o128b[:],
                         src_sb[:, dt * 512:(dt + 1) * 512],
                         start=(dt == 0), stop=(dt == NT - 1))
            t.wait_ge(sV.h, v_sq)
            ins = None
            for dt in range(NT):
                ins = t.matmul(self.p_st2[:], self.o128b[:],
                               w.sq[:, dt * 512:(dt + 1) * 512],
                               start=(dt == 0), stop=(dt == NT - 1))
            self.sMM.inc(ins)
        block.tensor(_)
        self.mark("p1sq", sMM)
        mm_st = sMM.n
        def _(v):
            v.wait_ge(sMM.h, mm_st)
            v.tensor_scalar(w.mean[:], self.p_st1[:], 1.0 / D, None, ALU.mult)
            v.tensor_scalar(w.var[:], self.p_st2[:], 1.0 / D, None, ALU.mult)
            v.tensor_tensor(w.r[:], w.mean[:], w.mean[:], ALU.mult)
            self.sV.inc(v.tensor_tensor(w.var[:], w.var[:], w.r[:], ALU.subtract))
        block.vector(_)
        self.mark("p_st", sV)
        v_var = sV.n
        def _(s):
            s.wait_ge(sV.h, v_var)
            s.activation(w.r[:], w.var[:], AF.Ln, bias=self.eps_sb[:])
            self.sA.inc(s.activation(w.r[:], w.r[:], AF.Exp, scale=-0.5))
        block.scalar(_)
        a_r = sA.n
        def _(v):
            v.wait_ge(sA.h, a_r)
            self.sV.inc(v.tensor_tensor(w.mr[:], w.mean[:], w.r[:], ALU.mult))
        block.vector(_)
        v_mr = sV.n
        def _(t):
            t.wait_ge(sV.h, v_mr)
            self.guard(t, "p_bc")
            t.matmul(self.p_bc[:], self.ones1f[:], w.r[:], start=True, stop=True)
            self.sMM.inc(t.matmul(self.p_bc2[:], self.ones1f[:], w.mr[:], start=True, stop=True))
        block.tensor(_)
        return sMM.n

    def ln_norm(self, v, out_ap, src_sb, tmp_sb, res_ap=None, out3d=True):
        bc = self.p_bc[:].rearrange('p (o m) -> p o m', o=1).broadcast_to([128, NT, 512])
        bc2 = self.p_bc2[:].rearrange('p (o m) -> p o m', o=1).broadcast_to([128, NT, 512])
        v3 = lambda ap: ap.rearrange("p (t m) -> p t m", t=NT)
        v.tensor_tensor(v3(tmp_sb[:]), v3(src_sb[:]), bc, ALU.mult)
        if res_ap is None:
            return v.tensor_tensor(v3(out_ap) if out3d else out_ap,
                                   v3(tmp_sb[:]), bc2, ALU.subtract)
        v.tensor_tensor(v3(tmp_sb[:]), v3(tmp_sb[:]), bc2, ALU.subtract)
        return v.tensor_tensor(out_ap, tmp_sb[:], res_ap, ALU.add)

    class W:
        pass

    def ln_scratch(self, tag):
        w = MK.W()
        w.sq = self.psb(f"{tag}_sq", [128, NT * 512], BF16)
        w.mean = self.psb(f"{tag}_mean", [1, 512], F32)
        w.var = self.psb(f"{tag}_var", [1, 512], F32)
        w.r = self.psb(f"{tag}_r", [1, 512], F32)
        w.mr = self.psb(f"{tag}_mr", [1, 512], F32)
        return w

    def p1_ln1(self, block):
        self.phase_open()
        xin = [self.psb(f"p1_xin{i}", [128, NT * 512], BF16) for i in range(2)]
        w = self.ln_scratch("p1")
        tln = w.sq
        lno = xin

        for c in range(8):
            def _(g):
                self.guard(g, f"p1x{c % 2}")
                self.sG.inc(g.dma_start(
                    out=xin[c % 2][:].rearrange("p (t m) -> p t m", t=NT),
                    in_=self.x_img[:, :, c * 512:(c + 1) * 512]), 16)
            block.gpsimd(_)
            ld = self.sG.n
            def _(v):
                v.wait_ge(self.sG.h, ld)
            block.vector(_)
            bc_ready = self.ln_stats(block, xin[c % 2], w, waits=[(self.sG, ld)])
            if self.dbg.get("p1_dbg") == "out":
                def _(y):
                    y.wait_ge(self.sMM.h, bc_ready)
                    for i, t_ in enumerate((w.mean, w.var, w.r, w.mr)):
                        self.sS.inc(y.dma_start(out=self.p1_dbg[c, i:i+1, :], in_=t_[:]), 16)
                block.sync(_)
            def _(v):
                v.wait_ge(self.sMM.h, bc_ready)
                self.sV.inc(self.ln_norm(v, lno[c % 2][:], xin[c % 2], tln))
            block.vector(_)
            self.mark("p_bc", self.sV)
            nrm = self.sV.n
            def _(y):
                y.wait_ge(self.sV.h, nrm)
                self.sS.inc(y.dma_start(
                    out=self.ln1_s[:, :, c * 512:(c + 1) * 512],
                    in_=lno[c % 2][:].rearrange("p (t m) -> p t m", t=NT)), 16)
            block.sync(_)
            self.mark(f"p1x{c % 2}", self.sS)
        self.phase_close(block)

    def p2_qkv(self, block):
        self.phase_open()
        sb = self.psb
        lnh = sb("p2_lnh", [128, NT * HALF], BF16)
        wqk = [sb(f"p2_wqk{i}", [128, NT * 128], BF16) for i in range(2)]
        wv = [sb(f"p2_wv{i}", [128, NT * 256], BF16) for i in range(2)]
        qko = [sb(f"p2_qko{i}", [128, 2048], BF16) for i in range(2)]
        vo = [sb(f"p2_vo{i}", [128, 256], BF16) for i in range(2)]

        for h in range(2):
            def _(y):
                self.guard(y, "p2lnh")
                self.sS.inc(y.dma_start(
                    out=lnh[:].rearrange("p (t m) -> p t m", t=NT),
                    in_=self.ln1_s[:, :, h * HALF:(h + 1) * HALF]), 16)
            block.sync(_)
            ln_ld = self.sS.n

            si = 0
            for wimg, bias, dst in ((self.wq_img, self.bq_sb, self.qt_s),
                                    (self.wk_img, self.bk_sb, self.kt_s)):
                for ot in range(NH):
                    def _(g):
                        self.guard(g, f"p2w{si % 2}")
                        self.sG.inc(g.dma_start(out=wqk[si % 2][:], in_=wimg[ot]), 16)
                    block.gpsimd(_)
                    w_ld = self.sG.n
                    banks = []
                    def _(t):
                        t.wait_ge(self.sG.h, w_ld)
                        t.wait_ge(self.sS.h, ln_ld)
                        for ch in range(4):
                            b = self.mm_bank(t)
                            banks.append(b)
                            ins = None
                            for dt in range(NT):
                                ins = t.matmul(self.p_mm[:, b * 512:(b + 1) * 512],
                                               wqk[si % 2][:, dt * 128:(dt + 1) * 128],
                                               lnh[:, dt * HALF + ch * 512: dt * HALF + (ch + 1) * 512],
                                               start=(dt == 0), stop=(dt == NT - 1))
                            self.sMM.inc(ins)
                    block.tensor(_)
                    self.mark(f"p2w{si % 2}", self.sMM)
                    mm_base = self.sMM.n - 4
                    def _(s):
                        self.guard(s, f"p2q{si % 2}")
                        for ch in range(4):
                            s.wait_ge(self.sMM.h, mm_base + ch + 1)
                            b = banks[ch]
                            self.sA.inc(s.activation(
                                qko[si % 2][:, ch * 512:(ch + 1) * 512],
                                self.p_mm[:, b * 512:(b + 1) * 512],
                                AF.Identity, bias=bias[:, ot:ot + 1]))
                            self.mark(f"bk{b}", self.sA)
                    block.scalar(_)
                    ev = self.sA.n
                    def _(y):
                        y.wait_ge(self.sA.h, ev)
                        self.sS.inc(y.dma_start(out=dst[ot, :, h * HALF:(h + 1) * HALF],
                                                in_=qko[si % 2][:]), 16)
                    block.sync(_)
                    self.mark(f"p2q{si % 2}", self.sS)
                    si += 1
            self.mark("p2lnh_qk", self.sMM)

            for oc in range(8):
                def _(g):
                    self.guard(g, f"p2v{oc % 2}")
                    self.sG.inc(g.dma_start(out=wv[oc % 2][:], in_=self.wv_img[oc]), 16)
                block.gpsimd(_)
                wv_ld = self.sG.n
                for tt in range(NH):
                    bb = []
                    def _(t):
                        if tt == 0:
                            t.wait_ge(self.sG.h, wv_ld)
                        b = self.mm_bank(t)
                        bb.append(b)
                        for dt in range(NT):
                            t.matmul(self.p_mm[:, b * 512:b * 512 + 256],
                                     lnh[:, dt * HALF + tt * 128: dt * HALF + tt * 128 + 128],
                                     wv[oc % 2][:, dt * 256:(dt + 1) * 256],
                                     start=(dt == 0), stop=False)
                        self.sMM.inc(t.matmul(self.p_mm[:, b * 512:b * 512 + 256],
                                              self.ones1b[:],
                                              self.bv_sb[:, oc * 256:(oc + 1) * 256],
                                              start=False, stop=True))
                    block.tensor(_)
                    mm_v = self.sMM.n
                    bi = (oc * NH + tt) % 2
                    def _(v):
                        v.wait_ge(self.sMM.h, mm_v)
                        self.guard(v, f"p2vo{bi}")
                        b = bb[0]
                        self.sV.inc(v.tensor_copy(vo[bi][:], self.p_mm[:, b * 512:b * 512 + 256]))
                        self.mark(f"bk{b}", self.sV)
                    block.vector(_)
                    vev = self.sV.n
                    def _(y):
                        y.wait_ge(self.sV.h, vev)
                        self.sS.inc(y.dma_start(
                            out=self.v_s[oc * 2:oc * 2 + 2, h * NH + tt, :, :].rearrange(
                                "hh p m -> p hh m"),
                            in_=vo[bi][:].rearrange("p (hh m) -> p hh m", hh=2)), 16)
                    block.sync(_)
                    self.mark(f"p2vo{bi}", self.sS)
                self.mark(f"p2v{oc % 2}", self.sMM)
            self.mark("p2lnh", self.sMM)
        self.phase_close(block)


    def p3_att(self, block, do_cc=True):
        SCALE = 0.08838834764831845
        self.phase_open()
        sb = self.psb
        kt_h = [sb(f"p3_kt{i}", [128, L], BF16) for i in range(2)]
        v_h = [sb(f"p3_v{i}", [128, L], BF16) for i in range(2)]
        q_h = [sb(f"p3_q{i}", [128, L], BF16) for i in range(2)]
        att = sb("p3_att", [128, 2048], BF16)
        acc = sb("p3_acc", [128, 512], F32)
        o_sb = sb("p3_o", [128, 512], BF16)
        rb_sb = sb("p3_rb", [128, 512], F32)
        rc_sb = sb("p3_rc", [1, 512], F32)
        mask = sb("p3_mask", [128, 4 * 512], BF16)
        dbg_o = sb("p3_dbgo", [128, 4 * 512], F32) if self.dbg.get("p3_dbg") == "out" else None
        def _(g):
            self.sG.inc(g.dma_start(
                out=mask[:].rearrange("p (r m) -> p r m", r=4), in_=self.mask_i), 16)
        block.gpsimd(_)
        mask_ld = self.sG.n
        def _(t):
            t.wait_ge(self.sG.h, mask_ld)
        block.tensor(_)

        for h in range(NH):
            def _(y):
                self.guard(y, f"p3h{h % 2}")
                self.sS.inc(y.dma_start(out=kt_h[h % 2][:], in_=self.kt_s[h]), 16)
                self.sS.inc(y.dma_start(out=q_h[h % 2][:], in_=self.qt_s[h]), 16)
                self.sS.inc(y.dma_start(
                    out=v_h[h % 2][:].rearrange("p (th m) -> p th m", th=32),
                    in_=self.v_s[h].rearrange("th p m -> p th m")), 16)
            block.sync(_)
            h_ld = self.sS.n
            def _(t):
                t.wait_ge(self.sS.h, h_ld)
            block.tensor(_)
            for qc in range(8):
                K = 4 * (qc + 1)
                G = K // 2
                exp_done = {}
                mask_done = {}
                av_done = {}
                for g in range(G + 1):
                    def _(t):
                        if g < G:
                            buf = g % 2
                            if g >= 2:
                                t.wait_ge(self.sA.h, exp_done[g - 2])
                            else:
                                self.guard(t, f"p3s{buf}")
                            for j in range(2):
                                kt = 2 * g + j
                                ins = t.matmul(self.p_mm[:, buf * 1024 + j * 512: buf * 1024 + (j + 1) * 512],
                                               kt_h[h % 2][:, kt * 128:(kt + 1) * 128],
                                               q_h[h % 2][:, qc * 512:(qc + 1) * 512],
                                               start=True, stop=True)
                            self.sMM.inc(ins)
                        if g >= 1:
                            gg = g - 1
                            t.wait_ge(self.sV.h, mask_done[gg])
                            if gg == 0:
                                self.guard(t, "p3av")
                            ins = None
                            for j in range(2):
                                kt = 2 * gg + j
                                ins = t.matmul(self.p_bc2[:],
                                               v_h[h % 2][:, kt * 128:(kt + 1) * 128],
                                               att[:, (gg % 2) * 1024 + j * 512: (gg % 2) * 1024 + (j + 1) * 512],
                                               start=(gg == 0 and j == 0),
                                               stop=(gg == G - 1 and j == 1))
                            self.sMM.inc(ins)
                            av_done[gg] = self.sMM.n
                    block.tensor(_)
                    if g < G:
                        mm_s = self.sMM.n
                        def _(s):
                            buf = g % 2
                            s.wait_ge(self.sMM.h, mm_s)
                            if g >= 2:
                                s.wait_ge(self.sV.h, mask_done[g - 2])
                                s.wait_ge(self.sMM.h, av_done[g - 2])
                            self.sA.inc(s.activation(att[:, buf * 1024:(buf + 1) * 1024],
                                                     self.p_mm[:, buf * 1024:(buf + 1) * 1024],
                                                     AF.Exp, scale=SCALE))
                            exp_done[g] = self.sA.n
                            self.mark(f"p3s{buf}", self.sA)
                        block.scalar(_)
                        def _(v):
                            buf = g % 2
                            v.wait_ge(self.sA.h, exp_done[g])
                            last = None
                            for j in range(2):
                                kt = 2 * g + j
                                r = kt - 4 * qc
                                ap = att[:, buf * 1024 + j * 512: buf * 1024 + (j + 1) * 512]
                                if 0 <= r < 4:
                                    v.tensor_tensor(ap, ap, mask[:, r * 512:(r + 1) * 512], ALU.mult)
                                if kt == 0:
                                    self.guard(v, "p3acc")
                                    last = v.tensor_copy(acc[:], ap)
                                else:
                                    last = v.tensor_tensor(acc[:], acc[:], ap, ALU.add)
                            self.sV.inc(last)
                            mask_done[g] = self.sV.n
                        block.vector(_)
                acc_done = self.sV.n
                def _(t):
                    t.wait_ge(self.sV.h, acc_done)
                    self.guard(t, "p3st")
                    self.sMM.inc(t.matmul(self.p_st1[:], self.o128f[:], acc[:],
                                          start=True, stop=True))
                    self.mark("p3acc", self.sMM)
                block.tensor(_)
                sum_mm = self.sMM.n
                def _(v):
                    v.wait_ge(self.sMM.h, sum_mm)
                    self.sV.inc(v.reciprocal(rc_sb[:], self.p_st1[:]))
                    self.mark("p3st", self.sV)
                block.vector(_)
                rc_done = self.sV.n
                def _(t):
                    t.wait_ge(self.sV.h, rc_done)
                    self.guard(t, "p3bc")
                    self.sMM.inc(t.matmul(self.p_bc[:], self.ones1f[:], rc_sb[:],
                                          start=True, stop=True))
                block.tensor(_)
                bc_mm = self.sMM.n
                av_mm = self.sMM.n
                dbgme = (self.dbg.get("p3_dbg") == "out" and h == 0 and qc == 0)
                def _(v):
                    v.wait_ge(self.sMM.h, bc_mm)
                    v.wait_ge(self.sMM.h, sum_mm)
                    self.guard(v, "p3o")
                    v.tensor_copy(rb_sb[:], self.p_bc[:])
                    if dbgme:
                        v.tensor_copy(dbg_o[:, 0:512], att[:, 0:512])
                        v.tensor_copy(dbg_o[:, 512:1024], acc[:])
                        v.tensor_copy(dbg_o[:, 1024:1536], self.p_bc2[:])
                        v.tensor_copy(dbg_o[:, 1536:2048], rb_sb[:])
                    self.sV.inc(v.tensor_tensor(o_sb[:], self.p_bc2[:], rb_sb[:], ALU.mult))
                    self.mark("p3bc", self.sV)
                    self.mark("p3av", self.sV)
                block.vector(_)
                if dbgme:
                    dbg_done = self.sV.n
                    def _(y):
                        y.wait_ge(self.sV.h, dbg_done)
                        self.sS.inc(y.dma_start(
                            out=self.p3_dbg.rearrange("n p m -> p n m"),
                            in_=dbg_o[:].rearrange("p (n m) -> p n m", n=4)), 16)
                    block.sync(_)
                o_done = self.sV.n
                def _(y):
                    y.wait_ge(self.sV.h, o_done)
                    self.sS.inc(y.dma_start(out=self.ao_cc[h, :, qc * 512:(qc + 1) * 512],
                                            in_=o_sb[:]), 16)
                    self.mark("p3o", self.sS)
                block.sync(_)
            self.mark(f"p3h{h % 2}", self.sMM)
            if do_cc and h % 4 == 3:
                k = h // 4
                st_now = self.sS.n
                def _(g):
                    g.wait_ge(self.sS.h, st_now)
                    self.sCC.inc(g.collective_compute(
                        "AllGather", ALU.bypass,
                        replica_groups=[[0, 1], [2, 3], [4, 5], [6, 7]],
                        ins=[self.ao_cc[4 * k:4 * (k + 1)].rearrange("h p m -> (h p) m")],
                        outs=[self.ao_g[k].rearrange("h p m -> (h p) m")]))
                    if not hasattr(self, "p3_cc"):
                        self.p3_cc = []
                    self.p3_cc.append(self.sCC.n)
                block.gpsimd(_)
        self.phase_close(block)


    def p4_wo(self, block, cc_counts=None):
        self.phase_open()
        sb = self.psb
        at_sb = sb("p4_at", [128, NT * HALF], BF16)
        cnd = [sb(f"p4_cnd{i}", [128, 8 * 512], BF16) for i in range(2)]
        tmpb = sb("p4_tmp", [128, 8 * 512], BF16)
        wos = [sb(f"p4_wo{i}", [128, NT * 128], BF16) for i in range(2)]
        yo = [sb(f"p4_yo{i}", [128, 2048], BF16) for i in range(2)]
        for k in range(4):
            for c in range(4):
                def _(y):
                    if cc_counts is not None:
                        y.wait_ge(self.sCC.h, cc_counts[k])
                    self.guard(y, "p4cnd")
                    self.sS.inc(y.dma_start(
                        out=cnd[0][:].rearrange("p (i m) -> p i m", i=8),
                        in_=self.ao_g[k][:, :, c * 512:(c + 1) * 512].rearrange(
                            "i p m -> p i m")), 16)
                    self.sS.inc(y.dma_start(
                        out=cnd[1][:].rearrange("p (i m) -> p i m", i=8),
                        in_=self.ao_g[k][:, :, HALF + c * 512: HALF + (c + 1) * 512].rearrange(
                            "i p m -> p i m")), 16)
                block.sync(_)
                c_ld = self.sS.n
                def _(v):
                    v.wait_ge(self.sS.h, c_ld)
                    v.tensor_scalar(tmpb[:], cnd[0][:], self.sel_sb[:, 0:1], None, ALU.mult)
                    at3 = at_sb[:].rearrange("p (dt m) -> p dt m", dt=NT)
                    ins = None
                    for s_ in range(2):
                        dst = at3[:, 16 * s_ + 4 * k: 16 * s_ + 4 * k + 4,
                                  c * 512:(c + 1) * 512]
                        src1 = cnd[1][:].rearrange("p (i m) -> p i m", i=8)[:, 4 * s_:4 * s_ + 4]
                        srct = tmpb[:].rearrange("p (i m) -> p i m", i=8)[:, 4 * s_:4 * s_ + 4]
                        ins = v.scalar_tensor_tensor(
                            dst, src1, self.sel_sb[:, 1:2], srct, ALU.mult, ALU.add)
                    self.sV.inc(ins)
                    self.mark("p4cnd", self.sV)
                block.vector(_)
        blend_done = self.sV.n
        def _(t):
            t.wait_ge(self.sV.h, blend_done)
        block.tensor(_)
        for ot in range(32):
            def _(g):
                self.guard(g, f"p4w{ot % 2}")
                self.sG.inc(g.dma_start(out=wos[ot % 2][:], in_=self.wo_img[ot]), 16)
            block.gpsimd(_)
            w_ld = self.sG.n
            banks = []
            def _(t):
                t.wait_ge(self.sG.h, w_ld)
                for ch in range(4):
                    b = self.mm_bank(t)
                    banks.append(b)
                    ins = None
                    for dt in range(NT):
                        ins = t.matmul(self.p_mm[:, b * 512:(b + 1) * 512],
                                       wos[ot % 2][:, dt * 128:(dt + 1) * 128],
                                       at_sb[:, dt * HALF + ch * 512: dt * HALF + (ch + 1) * 512],
                                       start=(dt == 0), stop=(dt == NT - 1))
                    self.sMM.inc(ins)
            block.tensor(_)
            self.mark(f"p4w{ot % 2}", self.sMM)
            mm_base = self.sMM.n - 4
            def _(s):
                self.guard(s, f"p4y{ot % 2}")
                for ch in range(4):
                    s.wait_ge(self.sMM.h, mm_base + ch + 1)
                    b = banks[ch]
                    self.sA.inc(s.activation(
                        yo[ot % 2][:, ch * 512:(ch + 1) * 512],
                        self.p_mm[:, b * 512:(b + 1) * 512],
                        AF.Identity, bias=self.bo_sb[:, ot:ot + 1]))
                    self.mark(f"bk{b}", self.sA)
            block.scalar(_)
            ev = self.sA.n
            def _(y):
                y.wait_ge(self.sA.h, ev)
                self.sS.inc(y.dma_start(out=self.y_s[:, ot, :], in_=yo[ot % 2][:]), 16)
            block.sync(_)
            self.mark(f"p4y{ot % 2}", self.sS)
        self.phase_close(block)

    def p56_ffn(self, block):
        self.p5_ln23(block)
        self.p6_ffn(block)

    def p5_ln23(self, block):
        self.phase_open()
        sb = self.psb
        rt_all = self.rt_all
        ics_sb = sb("p5_ics", [8, D], BF16)
        yc = sb("p5_yc", [128, NT * 512], BF16)
        xc = sb("p5_xc", [128, NT * 512], BF16)
        xs_c = sb("p5_xsc", [128, NT * 512], BF16)
        e8 = sb("p5_e8", [8, 512], F32)
        w = self.ln_scratch("p5")
        tln = w.sq
        xf = yc
        def _(g):
            self.sG.inc(g.dma_start(out=ics_sb[:], in_=self.ics_i), 16)
        block.gpsimd(_)
        cs_ld = self.sG.n
        for c in range(4):
            def _(y):
                self.guard(y, "p5yc")
                self.sS.inc(y.dma_start(
                    out=yc[:].rearrange("p (t m) -> p t m", t=NT),
                    in_=self.y_s[:, :, c * 512:(c + 1) * 512]), 16)
            block.sync(_)
            y_ld = self.sS.n
            def _(g):
                self.guard(g, "p5xc")
                self.sG.inc(g.dma_start(
                    out=xc[:].rearrange("p (t m) -> p t m", t=NT),
                    in_=self.x_own[:, :, c * 512:(c + 1) * 512]), 16)
            block.gpsimd(_)
            x_ld = self.sG.n
            def _(v):
                v.wait_ge(self.sS.h, y_ld)
            block.vector(_)
            bc1 = self.ln_stats(block, yc, w, waits=[(self.sS, y_ld)])
            def _(v):
                v.wait_ge(self.sMM.h, bc1)
                v.wait_ge(self.sG.h, x_ld)
                self.sV.inc(self.ln_norm(v, yc[:], yc, tln, res_ap=xc[:]))
                self.mark("p_bc", self.sV)
                self.mark("p5xc", self.sV)
            block.vector(_)
            x2_done = self.sV.n
            def _(y):
                y.wait_ge(self.sV.h, x2_done)
                self.sS.inc(y.dma_start(
                    out=self.x2_s[:, :, c * 512:(c + 1) * 512],
                    in_=yc[:].rearrange("p (t m) -> p t m", t=NT)), 16)
            block.sync(_)
            bc2 = self.ln_stats(block, yc, w, waits=[])
            def _(v):
                v.wait_ge(self.sMM.h, bc2)
                self.sV.inc(self.ln_norm(v, xf[:], yc, tln))
                self.mark("p_bc", self.sV)
            block.vector(_)
            xf_done = self.sV.n
            def _(t):
                t.wait_ge(self.sV.h, xf_done)
                t.wait_ge(self.sG.h, cs_ld)
                self.guard(t, "p5lg")
                ins = None
                for dt in range(NT):
                    ins = t.matmul(self.p_bc2[0:8, :], self.gw_sb[:, dt * 8:(dt + 1) * 8],
                                   xf[:, dt * 512:(dt + 1) * 512],
                                   start=(dt == 0), stop=(dt == NT - 1))
                self.sMM.inc(ins)
            block.tensor(_)
            lg_mm = self.sMM.n
            def _(s):
                s.wait_ge(self.sMM.h, lg_mm)
                self.guard(s, "p5e8")
                self.sA.inc(s.activation(e8[:], self.p_bc2[0:8, :], AF.Exp))
                self.mark("p5lg", self.sA)
            block.scalar(_)
            e8_done = self.sA.n
            def _(t):
                t.wait_ge(self.sA.h, e8_done)
                self.guard(t, "p5st")
                self.sMM.inc(t.matmul(self.p_st1[:], self.o128f[0:8, :], e8[:],
                                      start=True, stop=True))
            block.tensor(_)
            s8_mm = self.sMM.n
            def _(v):
                v.wait_ge(self.sMM.h, s8_mm)
                self.sV.inc(v.reciprocal(w.r[:], self.p_st1[:]))
                self.mark("p5st", self.sV)
            block.vector(_)
            rc8 = self.sV.n
            def _(t):
                t.wait_ge(self.sV.h, rc8)
                self.guard(t, "p5rb")
                self.sMM.inc(t.matmul(self.p_bc[0:8, :], self.ones1f[:, 0:8], w.r[:],
                                      start=True, stop=True))
            block.tensor(_)
            rb8 = self.sMM.n
            def _(v):
                v.wait_ge(self.sMM.h, rb8)
                self.sV.inc(v.tensor_tensor(rt_all[:, c * 512:(c + 1) * 512],
                                            e8[:], self.p_bc[0:8, :], ALU.mult))
                self.mark("p5rb", self.sV)
                self.mark("p5e8", self.sV)
            block.vector(_)
            rt_done = self.sV.n
            for dt in range(NT):
                def _(t):
                    if dt == 0:
                        t.wait_ge(self.sV.h, rt_done)
                        self.guard(t, "p5xsc")
                    b = self.mm_bank(t)
                    self.sMM.inc(t.matmul(self.p_mm[:, b * 512:(b + 1) * 512],
                                          ics_sb[:, dt * 128:(dt + 1) * 128],
                                          rt_all[0:8, c * 512:(c + 1) * 512],
                                          start=True, stop=True))
                    self._p5b = b
                block.tensor(_)
                is_mm = self.sMM.n
                def _(v):
                    v.wait_ge(self.sMM.h, is_mm)
                    b = self._p5b
                    self.sV.inc(v.tensor_tensor(
                        xs_c[:, dt * 512:(dt + 1) * 512],
                        xf[:, dt * 512:(dt + 1) * 512],
                        self.p_mm[:, b * 512:(b + 1) * 512], ALU.mult))
                    self.mark(f"bk{b}", self.sV)
                block.vector(_)
            xs_done = self.sV.n
            self.mark("p5yc", self.sV)
            self.mark("p5yc", self.sS, add=True)
            def _(y):
                y.wait_ge(self.sV.h, xs_done)
                self.sS.inc(y.dma_start(
                    out=self.xs_s[:, :, c * 512:(c + 1) * 512],
                    in_=xs_c[:].rearrange("p (t m) -> p t m", t=NT)), 16)
            block.sync(_)
            self.mark("p5xsc", self.sS)
        self.phase_close(block)

    def p6_ffn(self, block):
        self.phase_open()
        sb = self.psb
        xs_sb = sb("p6_xs", [128, NT * HALF], BF16)
        ocs_sb = sb("p6_ocs", [8, D], BF16)
        wfb = [sb(f"p6_wf{i}", [128, NT * 128], BF16) for i in range(2)]
        fo = [sb(f"p6_fo{i}", [128, 2048], BF16) for i in range(2)]
        ftmp = sb("p6_ftmp", [128, 512], BF16)
        rt_all = self.rt_all
        def _(g):
            self.sG.inc(g.dma_start(out=ocs_sb[:], in_=self.ocs_i), 16)
        block.gpsimd(_)
        def _(y):
            self.sS.inc(y.dma_start(
                out=xs_sb[:].rearrange("p (t m) -> p t m", t=NT),
                in_=self.xs_s[:, :, :]), 16)
        block.sync(_)
        xs_ld = self.sS.n
        oc_ld = self.sG.n
        for ot in range(32):
            def _(g):
                self.guard(g, f"p6sg{ot % 2}")
                self.sG.inc(g.dma_start(out=wfb[ot % 2][:], in_=self.wf_img[ot]), 16)
            block.gpsimd(_)
            w_ld = self.sG.n
            def _(s):
                s.wait_ge(self.sG.h, w_ld)
                self.sA.inc(s.activation(wfb[ot % 2][:], wfb[ot % 2][:], AF.Sign))
            block.scalar(_)
            sg_done = self.sA.n
            for c in range(4):
                banks = []
                def _(t):
                    if c == 0:
                        t.wait_ge(self.sA.h, sg_done)
                        if ot == 0:
                            t.wait_ge(self.sS.h, xs_ld)
                            t.wait_ge(self.sG.h, oc_ld)
                    b = self.mm_bank(t)
                    banks.append(b)
                    ins = None
                    for dt in range(NT):
                        ins = t.matmul(self.p_mm[:, b * 512:(b + 1) * 512],
                                       wfb[ot % 2][:, dt * 128:(dt + 1) * 128],
                                       xs_sb[:, dt * HALF + c * 512: dt * HALF + (c + 1) * 512],
                                       start=(dt == 0), stop=(dt == NT - 1))
                    self.sMM.inc(ins)
                    if c == 3:
                        self.mark(f"p6sg{ot % 2}", self.sMM)
                    self.guard(t, "p6os")
                    self.sMM.inc(t.matmul(self.p_bc[:], ocs_sb[:, ot * 128:(ot + 1) * 128],
                                          rt_all[0:8, c * 512:(c + 1) * 512],
                                          start=True, stop=True))
                block.tensor(_)
                mm_c = self.sMM.n
                def _(s):
                    s.wait_ge(self.sMM.h, mm_c - 1)
                    self.guard(s, "p6t")
                    b = banks[0]
                    self.sA.inc(s.activation(ftmp[:], self.p_mm[:, b * 512:(b + 1) * 512],
                                             AF.Identity, bias=self.bf_sb[:, ot:ot + 1]))
                    self.mark(f"bk{b}", self.sA)
                block.scalar(_)
                ft_done = self.sA.n
                def _(v):
                    v.wait_ge(self.sA.h, ft_done)
                    v.wait_ge(self.sMM.h, mm_c)
                    self.guard(v, f"p6fo{ot % 2}")
                    self.sV.inc(v.tensor_tensor(fo[ot % 2][:, c * 512:(c + 1) * 512],
                                                ftmp[:], self.p_bc[:], ALU.mult))
                    self.mark("p6os", self.sV)
                    self.mark("p6t", self.sV)
                block.vector(_)
            fo_done = self.sV.n
            def _(y):
                y.wait_ge(self.sV.h, fo_done)
                self.sS.inc(y.dma_start(out=self.fo_s[:, ot, :], in_=fo[ot % 2][:]), 16)
            block.sync(_)
            self.mark(f"p6fo{ot % 2}", self.sS)
        self.phase_close(block)

    def p7_out(self, block):
        self.phase_open()
        foc = self.psb("p7_fo", [128, NT * 512], BF16)
        x2c = self.psb("p7_x2", [128, NT * 512], BF16)
        outc = self.psb("p7_out", [128, NT * 512], F32)
        w = self.ln_scratch("p7")
        tln = w.sq
        for c in range(4):
            def _(y):
                self.guard(y, "p7f")
                self.sS.inc(y.dma_start(
                    out=foc[:].rearrange("p (t m) -> p t m", t=NT),
                    in_=self.fo_s[:, :, c * 512:(c + 1) * 512]), 16)
                self.sS.inc(y.dma_start(
                    out=x2c[:].rearrange("p (t m) -> p t m", t=NT),
                    in_=self.x2_s[:, :, c * 512:(c + 1) * 512]), 16)
            block.sync(_)
            ld = self.sS.n
            bc = self.ln_stats(block, foc, w, waits=[(self.sS, ld)])
            def _(v):
                v.wait_ge(self.sMM.h, bc)
                v.wait_ge(self.sS.h, ld)
                self.guard(v, "p7o")
                self.sV.inc(self.ln_norm(v, outc[:], foc, tln, res_ap=x2c[:]))
                self.mark("p_bc", self.sV)
                self.mark("p7f", self.sV)
            block.vector(_)
            o_done = self.sV.n
            def _(y):
                y.wait_ge(self.sV.h, o_done)
                self.sS.inc(y.dma_start(
                    out=self.out_img[:, :, c * 512:(c + 1) * 512],
                    in_=outc[:].rearrange("p (t m) -> p t m", t=NT)), 16)
            block.sync(_)
            self.mark("p7o", self.sS)
        self.phase_close(block)

    def finalize(self):
        self.ctx.close()
        return self.nc


def build(phases=("p0", "p1", "p2"), dbg_modes=None):
    mk = MK(dbg_modes)
    mk.declare()
    with mk.nc.Block() as block:
        if "p0" in phases:
            mk.p0_consts(block)
        if "p1" in phases:
            mk.p1_ln1(block)
        if "p2" in phases:
            mk.p2_qkv(block)
        if "p3" in phases:
            mk.p3_att(block, do_cc=("cc" in phases))
        if "p4" in phases:
            mk.p4_wo(block, cc_counts=mk.p3_cc if ("cc" in phases) else None)
        if "p56" in phases:
            mk.p56_ffn(block)
        if "p7" in phases:
            mk.p7_out(block)
        def _(y):
            y.wait_ge(mk.sS.h, mk.sS.n)
            y.wait_ge(mk.sG.h, mk.sG.n)
        block.sync(_)
    return mk


def host_inputs(core, x, wq, bq, wk, bk, wv, bv, wo, bo, wf, bf, gw, ics, ocs):
    b, hh = core // 2, core % 2
    sl = slice(hh * 2048, (hh + 1) * 2048)
    xT = np.ascontiguousarray(x[b].T)
    d = {}
    d["x_img"] = np.ascontiguousarray(xT.reshape(NT, 128, L).transpose(1, 0, 2))
    d["x_own"] = np.ascontiguousarray(xT[:, sl].reshape(NT, 128, HALF).transpose(1, 0, 2))
    def wimg_qk(w):
        wt = np.ascontiguousarray(w.T[:, sl])
        return np.ascontiguousarray(
            wt.reshape(NT, 128, NH, 128).transpose(2, 1, 0, 3).reshape(NH, 128, NT * 128))
    d["wq_img"] = wimg_qk(wq)
    d["wk_img"] = wimg_qk(wk)
    wvt = np.ascontiguousarray(wv.T[:, sl])
    d["wv_img"] = np.ascontiguousarray(
        wvt.reshape(NT, 128, 8, 256).transpose(2, 1, 0, 3).reshape(8, 128, NT * 256))
    def wimg_full(w):
        wt = np.ascontiguousarray(w.T)
        return np.ascontiguousarray(
            wt.reshape(NT, 128, 32, 128).transpose(2, 1, 0, 3).reshape(32, 128, NT * 128))
    d["wo_img"] = wimg_full(wo)
    d["wf_img"] = wimg_full(wf)
    d["bq_i"] = np.ascontiguousarray(bq[sl].reshape(NH, 128).T)
    d["bk_i"] = np.ascontiguousarray(bk[sl].reshape(NH, 128).T)
    d["bv_i"] = np.ascontiguousarray(bv[sl].reshape(1, NH * 128))
    d["bo_i"] = np.ascontiguousarray(bo.reshape(32, 128).T)
    d["bf_i"] = np.ascontiguousarray(bf.reshape(32, 128).T)
    d["gw_img"] = np.ascontiguousarray(gw.T.reshape(NT, 128, 8).transpose(1, 0, 2))
    d["ics_i"] = np.ascontiguousarray(ics)
    d["ocs_i"] = np.ascontiguousarray(ocs)
    k = np.arange(128)[:, None]
    j = np.arange(512)[None, :]
    masks = np.stack([(j >= k + 128 * r).astype(np.float32) for r in range(4)], axis=1)
    d["mask_i"] = np.ascontiguousarray(masks)
    sel = np.zeros((128, 2), np.float32)
    sel[:, 0] = 1.0 - hh
    sel[:, 1] = hh
    d["sel_i"] = sel
    return d


from concourse.bass_utils import run_bass_kernel_spmd


def kernel(L0_attn_x, L0_attn_Wq, L0_attn_bq, L0_attn_Wk, L0_attn_bk,
           L0_attn_Wv, L0_attn_bv, L0_attn_Wo, L0_attn_bo,
           L0_ffn_weight, L0_ffn_bias, L0_ffn_gate_w,
           L0_ffn_in_channel_scale, L0_ffn_out_channel_scale):
    x = np.asarray(L0_attn_x, dtype=np.float32)
    args = [np.asarray(a, dtype=np.float32) for a in
            (L0_attn_Wq, L0_attn_bq, L0_attn_Wk, L0_attn_bk, L0_attn_Wv,
             L0_attn_bv, L0_attn_Wo, L0_attn_bo, L0_ffn_weight, L0_ffn_bias,
             L0_ffn_gate_w, L0_ffn_in_channel_scale, L0_ffn_out_channel_scale)]
    (wq, bq, wk, bk, wv, bv, wo, bo, wf, bf, gw, ics, ocs) = args

    mk = build(phases=("p0", "p1", "p2", "p3", "cc", "p4", "p56", "p7"))
    nc = mk.finalize()
    in_maps = [host_inputs(c, x, wq, bq, wk, bk, wv, bv, wo, bo, wf, bf, gw, ics, ocs)
               for c in range(8)]
    res = run_bass_kernel_spmd(nc, in_maps, core_ids=list(range(8)))
    out = np.empty((4, L, D), np.float32)
    for c in range(8):
        b, hh = c // 2, c % 2
        img = np.asarray(res.results[c]["out_img"], dtype=np.float32)
        outT = img.reshape(128, NT, HALF).transpose(1, 0, 2).reshape(D, HALF)
        out[b, hh * HALF:(hh + 1) * HALF, :] = outT.T
    return out


# version 02
# speedup vs baseline: 1.3963x; 1.3963x over previous
import numpy as np
import concourse.bass as bass
import concourse.mybir as mybir
from contextlib import ExitStack

F32 = mybir.dt.float32
BF16 = mybir.dt.bfloat16
AF = mybir.ActivationFunctionType
ALU = mybir.AluOpType

D = 4096
NT = 32
L = 4096
NH = 16
HALF = 2048
EPS = 1e-5


class Ctr:
    def __init__(self, ctx, nc, name):
        self.h = ctx.enter_context(nc.semaphore(name))
        self.n = 0

    def inc(self, instr, k=1):
        instr.then_inc(self.h, k)
        self.n += k
        return instr


class MK:
    def __init__(self, dbg_modes=None):
        self.dbg = dbg_modes or {}
        self.nc = bass.Bass("TRN2", target_bir_lowering=False)
        self.ctx = ExitStack()
        self.pctx = None
        self.res = {}

    def mark(self, name, ctr, lag=0, add=False):
        if add and name in self.res:
            self.res[name][id(ctr)] = (ctr, ctr.n - lag)
        else:
            self.res[name] = {id(ctr): (ctr, ctr.n - lag)}

    def guard(self, eng, name):
        for ctr, n in self.res.get(name, {}).values():
            if n > 0:
                eng.wait_ge(ctr.h, n)

    def dram(self, name, shape, dtype, kind="Internal"):
        kind = {"in": "ExternalInput", "out": "ExternalOutput"}.get(self.dbg.get(name), kind)
        return self.nc.dram_tensor(name, shape, dtype, kind=kind).ap()

    def sb(self, name, shape, dtype):
        return self.ctx.enter_context(self.nc.sbuf_tensor(name, shape, dtype))

    def psb(self, name, shape, dtype):
        return self.pctx.enter_context(self.nc.sbuf_tensor(name, shape, dtype))

    def phase_open(self):
        assert self.pctx is None
        self.pctx = ExitStack()

    def phase_close(self, block):
        self.barrier(block)
        self.pctx.close()
        self.pctx = None

    def ps(self, name, shape):
        return self.ctx.enter_context(self.nc.psum_tensor(name, shape, F32))

    def declare(self):
        dram, sb = self.dram, self.sb
        self.x_img = dram("x_img", [128, NT, L], F32, kind="ExternalInput")
        self.x_own = dram("x_own", [128, NT, HALF], F32, kind="ExternalInput")
        self.wq_img = dram("wq_img", [NH, 128, NT * 128], F32, kind="ExternalInput")
        self.wk_img = dram("wk_img", [NH, 128, NT * 128], F32, kind="ExternalInput")
        self.wv_img = dram("wv_img", [8, 128, NT * 256], F32, kind="ExternalInput")
        self.wo_img = dram("wo_img", [32, 128, NT * 128], F32, kind="ExternalInput")
        self.wf_img = dram("wf_img", [32, 128, NT * 128], F32, kind="ExternalInput")
        self.bq_i = dram("bq_i", [128, NH], F32, kind="ExternalInput")
        self.bk_i = dram("bk_i", [128, NH], F32, kind="ExternalInput")
        self.bv_i = dram("bv_i", [1, NH * 128], F32, kind="ExternalInput")
        self.bo_i = dram("bo_i", [128, 32], F32, kind="ExternalInput")
        self.bf_i = dram("bf_i", [128, 32], F32, kind="ExternalInput")
        self.gw_img = dram("gw_img", [128, NT, 8], F32, kind="ExternalInput")
        self.ics_i = dram("ics_i", [8, D], F32, kind="ExternalInput")
        self.ocs_i = dram("ocs_i", [8, D], F32, kind="ExternalInput")
        self.mask_i = dram("mask_i", [128, 4, 512], F32, kind="ExternalInput")
        self.sel_i = dram("sel_i", [128, 2], F32, kind="ExternalInput")
        self.out_img = dram("out_img", [128, NT, HALF], F32, kind="ExternalOutput")
        self.ln1_s = dram("ln1_s", [128, NT, L], BF16)
        self.qt_s = dram("qt_s", [NH, 128, L], BF16)
        self.kt_s = dram("kt_s", [NH, 128, L], BF16)
        self.v_s = dram("v_s", [NH, 32, 128, 128], BF16)
        self.ao_cc = dram("ao_cc", [NH, 128, L], BF16)
        self.ao_g = [dram(f"ao_g{k}", [8, 128, L], BF16) for k in range(4)]
        self.y_s = dram("y_s", [128, NT, HALF], BF16)
        self.x2_s = dram("x2_s", [128, NT, HALF], BF16)
        self.fo_s = dram("fo_s", [128, NT, HALF], BF16)
        self.xs_s = dram("xs_s", [128, NT, HALF], BF16)
        self.p1_dbg = dram("p1_dbg", [8, 4, 512], F32)
        self.p3_dbg = dram("p3_dbg", [4, 128, 512], F32)
        self.bq_sb = sb("bq_sb", [128, NH], F32)
        self.bk_sb = sb("bk_sb", [128, NH], F32)
        self.bv_sb = sb("bv_sb", [1, NH * 128], BF16)
        self.bo_sb = sb("bo_sb", [128, 32], F32)
        self.bf_sb = sb("bf_sb", [128, 32], F32)
        self.gw_sb = sb("gw_sb", [128, NT * 8], BF16)
        self.sel_sb = sb("sel_sb", [128, 2], F32)
        self.eps_sb = sb("eps_sb", [1, 1], F32)
        self.ones1f = sb("ones1f", [1, 128], F32)
        self.ones1b = sb("ones1b", [1, 128], BF16)
        self.o128b = sb("o128b", [128, 1], BF16)
        self.o128f = sb("o128f", [128, 1], F32)
        self.scrap1 = sb("scrap1", [1, 8], F32)
        self.scrap2 = sb("scrap2", [1, 8], F32)
        self.scrap3 = sb("scrap3", [1, 8], F32)
        self.rt_all = sb("rt_all", [8, 4 * 512], BF16)
        self.p_mm = self.ps("p_mm", [128, 2048])
        self.p_bc = self.ps("p_bc", [128, 512])
        self.p_bc2 = self.ps("p_bc2", [128, 512])
        self.p_st1 = self.ps("p_st1", [1, 512])
        self.p_d = self.ps("p_d", [128, 512])
        self.gidx = 0
        ctx, nc = self.ctx, self.nc
        self.sG = Ctr(ctx, nc, "sG")
        self.sS = Ctr(ctx, nc, "sS")
        self.sMM = Ctr(ctx, nc, "sMM")
        self.sV = Ctr(ctx, nc, "sV")
        self.sA = Ctr(ctx, nc, "sA")
        self.sCC = Ctr(ctx, nc, "sCC")
        self.bar = Ctr(ctx, nc, "bar")

    def mm_bank(self, t):
        b = self.gidx % 4
        self.gidx += 1
        self.guard(t, f"bk{b}")
        return b

    def barrier(self, block):
        def _(v):
            self.sV.inc(v.tensor_copy(self.scrap2[:], self.scrap1[:]))
        block.vector(_)
        def _(s):
            self.sA.inc(s.copy(self.scrap3[:], self.scrap1[:]))
        block.scalar(_)
        waits = [(self.sMM, self.sMM.n), (self.sV, self.sV.n), (self.sA, self.sA.n),
                 (self.sS, self.sS.n), (self.sG, self.sG.n), (self.sCC, self.sCC.n)]
        for deco in (block.tensor, block.vector, block.scalar, block.gpsimd, block.sync):
            def _(eng):
                for ctr, n in waits:
                    if n > 0:
                        eng.wait_ge(ctr.h, n)
            deco(_)

    def p0_consts(self, block):
        def _(g):
            for dst, src in ((self.bq_sb[:], self.bq_i), (self.bk_sb[:], self.bk_i),
                             (self.bv_sb[:], self.bv_i), (self.bo_sb[:], self.bo_i),
                             (self.bf_sb[:], self.bf_i),
                             (self.gw_sb[:].rearrange("p (t e) -> p t e", t=NT), self.gw_img),
                             (self.sel_sb[:], self.sel_i)):
                self.sG.inc(g.dma_start(out=dst, in_=src), 16)
        block.gpsimd(_)
        def _(v):
            v.memset(self.eps_sb[:], EPS)
            v.memset(self.ones1f[:], 1.0)
            v.memset(self.ones1b[:], 1.0)
            v.memset(self.o128b[:], 1.0)
            v.memset(self.scrap1[:], 0.0)
            self.sV.inc(v.memset(self.o128f[:], 1.0))
        block.vector(_)
        def _(s):
            from concourse.hw_specs import get_activation_tables
            for idx, (nm, funcs) in enumerate(get_activation_tables(self.nc.m.arch).items()):
                if nm == "natural_log_exp_and_others":
                    s.add_instruction(mybir.InstLoadActFuncSet(
                        name=self.nc.get_next_instruction_name(),
                        act_func_set_id=idx, ins=[], outs=[]))
                    break
        block.scalar(_)
        self.barrier(block)

    def ln_stats(self, block, src_sb, w, waits=()):
        sV, sA, sMM = self.sV, self.sA, self.sMM
        def _(v):
            for ctr, n in waits:
                v.wait_ge(ctr.h, n)
            self.guard(v, "p1sq")
            self.sV.inc(v.tensor_tensor(w.sq[:], src_sb[:], src_sb[:], ALU.mult))
        block.vector(_)
        v_sq = sV.n
        def _(t):
            for ctr, n in waits:
                t.wait_ge(ctr.h, n)
            self.guard(t, "p_st")
            for dt in range(NT):
                t.matmul(self.p_st1[:], self.o128b[:],
                         src_sb[:, dt * 512:(dt + 1) * 512],
                         start=(dt == 0), stop=(dt == NT - 1))
            t.wait_ge(sV.h, v_sq)
            ins = None
            for dt in range(NT):
                ins = t.matmul(self.p_d[0:1, :], self.o128b[:],
                               w.sq[:, dt * 512:(dt + 1) * 512],
                               start=(dt == 0), stop=(dt == NT - 1))
            self.sMM.inc(ins)
        block.tensor(_)
        self.mark("p1sq", sMM)
        mm_st = sMM.n
        def _(v):
            v.wait_ge(sMM.h, mm_st)
            v.tensor_scalar(w.mean[:], self.p_st1[:], 1.0 / D, None, ALU.mult)
            v.tensor_scalar(w.var[:], self.p_d[0:1, :], 1.0 / D, None, ALU.mult)
            v.tensor_tensor(w.r[:], w.mean[:], w.mean[:], ALU.mult)
            self.sV.inc(v.tensor_tensor(w.var[:], w.var[:], w.r[:], ALU.subtract))
        block.vector(_)
        self.mark("p_st", sV)
        v_var = sV.n
        def _(s):
            s.wait_ge(sV.h, v_var)
            s.activation(w.r[:], w.var[:], AF.Ln, bias=self.eps_sb[:])
            self.sA.inc(s.activation(w.r[:], w.r[:], AF.Exp, scale=-0.5))
        block.scalar(_)
        a_r = sA.n
        def _(v):
            v.wait_ge(sA.h, a_r)
            self.sV.inc(v.tensor_tensor(w.mr[:], w.mean[:], w.r[:], ALU.mult))
        block.vector(_)
        v_mr = sV.n
        def _(t):
            t.wait_ge(sV.h, v_mr)
            self.guard(t, "p_bc")
            t.matmul(self.p_bc[:], self.ones1f[:], w.r[:], start=True, stop=True)
            self.sMM.inc(t.matmul(self.p_bc2[:], self.ones1f[:], w.mr[:], start=True, stop=True))
        block.tensor(_)
        return sMM.n

    def ln_norm(self, v, out_ap, src_sb, tmp_sb, res_ap=None, out3d=True):
        bc = self.p_bc[:].rearrange('p (o m) -> p o m', o=1).broadcast_to([128, NT, 512])
        bc2 = self.p_bc2[:].rearrange('p (o m) -> p o m', o=1).broadcast_to([128, NT, 512])
        v3 = lambda ap: ap.rearrange("p (t m) -> p t m", t=NT)
        v.tensor_tensor(v3(tmp_sb[:]), v3(src_sb[:]), bc, ALU.mult)
        if res_ap is None:
            return v.tensor_tensor(v3(out_ap) if out3d else out_ap,
                                   v3(tmp_sb[:]), bc2, ALU.subtract)
        v.tensor_tensor(v3(tmp_sb[:]), v3(tmp_sb[:]), bc2, ALU.subtract)
        return v.tensor_tensor(out_ap, tmp_sb[:], res_ap, ALU.add)

    class W:
        pass

    def ln_scratch(self, tag):
        w = MK.W()
        w.sq = self.psb(f"{tag}_sq", [128, NT * 512], BF16)
        w.mean = self.psb(f"{tag}_mean", [1, 512], F32)
        w.var = self.psb(f"{tag}_var", [1, 512], F32)
        w.r = self.psb(f"{tag}_r", [1, 512], F32)
        w.mr = self.psb(f"{tag}_mr", [1, 512], F32)
        return w

    def p1_ln1(self, block):
        self.phase_open()
        xin = [self.psb(f"p1_xin{i}", [128, NT * 512], BF16) for i in range(2)]
        w = self.ln_scratch("p1")
        tln = w.sq
        lno = xin

        for c in range(8):
            def _(g):
                self.guard(g, f"p1x{c % 2}")
                self.sG.inc(g.dma_start(
                    out=xin[c % 2][:].rearrange("p (t m) -> p t m", t=NT),
                    in_=self.x_img[:, :, c * 512:(c + 1) * 512]), 16)
            block.gpsimd(_)
            ld = self.sG.n
            def _(v):
                v.wait_ge(self.sG.h, ld)
            block.vector(_)
            bc_ready = self.ln_stats(block, xin[c % 2], w, waits=[(self.sG, ld)])
            if self.dbg.get("p1_dbg") == "out":
                def _(y):
                    y.wait_ge(self.sMM.h, bc_ready)
                    for i, t_ in enumerate((w.mean, w.var, w.r, w.mr)):
                        self.sS.inc(y.dma_start(out=self.p1_dbg[c, i:i+1, :], in_=t_[:]), 16)
                block.sync(_)
            def _(v):
                v.wait_ge(self.sMM.h, bc_ready)
                self.sV.inc(self.ln_norm(v, lno[c % 2][:], xin[c % 2], tln))
            block.vector(_)
            self.mark("p_bc", self.sV)
            nrm = self.sV.n
            def _(y):
                y.wait_ge(self.sV.h, nrm)
                self.sS.inc(y.dma_start(
                    out=self.ln1_s[:, :, c * 512:(c + 1) * 512],
                    in_=lno[c % 2][:].rearrange("p (t m) -> p t m", t=NT)), 16)
            block.sync(_)
            self.mark(f"p1x{c % 2}", self.sS)
        self.phase_close(block)

    def p2_qkv(self, block):
        self.phase_open()
        sb = self.psb
        lnh = sb("p2_lnh", [128, NT * HALF], BF16)
        wqk = [sb(f"p2_wqk{i}", [128, NT * 128], BF16) for i in range(2)]
        wv = [sb(f"p2_wv{i}", [128, NT * 256], BF16) for i in range(2)]
        qko = [sb(f"p2_qko{i}", [128, 2048], BF16) for i in range(2)]
        vo = [sb(f"p2_vo{i}", [128, 256], BF16) for i in range(2)]

        for h in range(2):
            def _(y):
                self.guard(y, "p2lnh")
                self.sS.inc(y.dma_start(
                    out=lnh[:].rearrange("p (t m) -> p t m", t=NT),
                    in_=self.ln1_s[:, :, h * HALF:(h + 1) * HALF]), 16)
            block.sync(_)
            ln_ld = self.sS.n

            si = 0
            for wimg, bias, dst in ((self.wq_img, self.bq_sb, self.qt_s),
                                    (self.wk_img, self.bk_sb, self.kt_s)):
                for ot in range(NH):
                    def _(g):
                        self.guard(g, f"p2w{si % 2}")
                        self.sG.inc(g.dma_start(out=wqk[si % 2][:], in_=wimg[ot]), 16)
                    block.gpsimd(_)
                    w_ld = self.sG.n
                    banks = []
                    def _(t):
                        t.wait_ge(self.sG.h, w_ld)
                        t.wait_ge(self.sS.h, ln_ld)
                        for ch in range(4):
                            b = self.mm_bank(t)
                            banks.append(b)
                            ins = None
                            for dt in range(NT):
                                ins = t.matmul(self.p_mm[:, b * 512:(b + 1) * 512],
                                               wqk[si % 2][:, dt * 128:(dt + 1) * 128],
                                               lnh[:, dt * HALF + ch * 512: dt * HALF + (ch + 1) * 512],
                                               start=(dt == 0), stop=(dt == NT - 1))
                            self.sMM.inc(ins)
                    block.tensor(_)
                    self.mark(f"p2w{si % 2}", self.sMM)
                    mm_base = self.sMM.n - 4
                    def _(s):
                        self.guard(s, f"p2q{si % 2}")
                        for ch in range(4):
                            s.wait_ge(self.sMM.h, mm_base + ch + 1)
                            b = banks[ch]
                            self.sA.inc(s.activation(
                                qko[si % 2][:, ch * 512:(ch + 1) * 512],
                                self.p_mm[:, b * 512:(b + 1) * 512],
                                AF.Identity, bias=bias[:, ot:ot + 1]))
                            self.mark(f"bk{b}", self.sA)
                    block.scalar(_)
                    ev = self.sA.n
                    def _(y):
                        y.wait_ge(self.sA.h, ev)
                        self.sS.inc(y.dma_start(out=dst[ot, :, h * HALF:(h + 1) * HALF],
                                                in_=qko[si % 2][:]), 16)
                    block.sync(_)
                    self.mark(f"p2q{si % 2}", self.sS)
                    si += 1
            self.mark("p2lnh_qk", self.sMM)

            for oc in range(8):
                def _(g):
                    self.guard(g, f"p2v{oc % 2}")
                    self.sG.inc(g.dma_start(out=wv[oc % 2][:], in_=self.wv_img[oc]), 16)
                block.gpsimd(_)
                wv_ld = self.sG.n
                for tt in range(NH):
                    bb = []
                    def _(t):
                        if tt == 0:
                            t.wait_ge(self.sG.h, wv_ld)
                        b = self.mm_bank(t)
                        bb.append(b)
                        for dt in range(NT):
                            t.matmul(self.p_mm[:, b * 512:b * 512 + 256],
                                     lnh[:, dt * HALF + tt * 128: dt * HALF + tt * 128 + 128],
                                     wv[oc % 2][:, dt * 256:(dt + 1) * 256],
                                     start=(dt == 0), stop=False)
                        self.sMM.inc(t.matmul(self.p_mm[:, b * 512:b * 512 + 256],
                                              self.ones1b[:],
                                              self.bv_sb[:, oc * 256:(oc + 1) * 256],
                                              start=False, stop=True))
                    block.tensor(_)
                    mm_v = self.sMM.n
                    bi = (oc * NH + tt) % 2
                    def _(v):
                        v.wait_ge(self.sMM.h, mm_v)
                        self.guard(v, f"p2vo{bi}")
                        b = bb[0]
                        self.sV.inc(v.tensor_copy(vo[bi][:], self.p_mm[:, b * 512:b * 512 + 256]))
                        self.mark(f"bk{b}", self.sV)
                    block.vector(_)
                    vev = self.sV.n
                    def _(y):
                        y.wait_ge(self.sV.h, vev)
                        self.sS.inc(y.dma_start(
                            out=self.v_s[oc * 2:oc * 2 + 2, h * NH + tt, :, :].rearrange(
                                "hh p m -> p hh m"),
                            in_=vo[bi][:].rearrange("p (hh m) -> p hh m", hh=2)), 16)
                    block.sync(_)
                    self.mark(f"p2vo{bi}", self.sS)
                self.mark(f"p2v{oc % 2}", self.sMM)
            self.mark("p2lnh", self.sMM)
        self.phase_close(block)


    def p3_att(self, block, do_cc=True):
        SCALE = 0.08838834764831845
        self.phase_open()
        sb = self.psb
        kt_h = [sb(f"p3_kt{i}", [128, L], BF16) for i in range(2)]
        v_h = [sb(f"p3_v{i}", [128, L], BF16) for i in range(2)]
        q_h = [sb(f"p3_q{i}", [128, L], BF16) for i in range(2)]
        att = sb("p3_att", [128, 3 * 1024], BF16)
        o_sb = [sb(f"p3_o{i}", [128, 512], BF16) for i in range(2)]
        rb_sb = sb("p3_rb", [128, 512], BF16)
        rc_sb = sb("p3_rc", [1, 512], F32)
        rc_b = sb("p3_rcb", [1, 512], BF16)
        mask = sb("p3_mask", [128, 4 * 512], BF16)
        def _(g):
            self.sG.inc(g.dma_start(
                out=mask[:].rearrange("p (r m) -> p r m", r=4), in_=self.mask_i), 16)
        block.gpsimd(_)
        mask_ld = self.sG.n
        def _(t):
            t.wait_ge(self.sG.h, mask_ld)
        block.tensor(_)

        def load_head(h):
            def _(y):
                self.guard(y, f"p3h{h % 2}")
                self.sS.inc(y.dma_start(out=kt_h[h % 2][:], in_=self.kt_s[h]), 16)
                self.sS.inc(y.dma_start(out=q_h[h % 2][:], in_=self.qt_s[h]), 16)
                self.sS.inc(y.dma_start(
                    out=v_h[h % 2][:].rearrange("p (th m) -> p th m", th=32),
                    in_=self.v_s[h].rearrange("th p m -> p th m")), 16)
            block.sync(_)
            return self.sS.n

        h_ld = {0: load_head(0)}
        pending = []

        for h in range(NH):
            if h + 1 < NH:
                h_ld[h + 1] = load_head(h + 1)
            def _(t):
                t.wait_ge(self.sS.h, h_ld[h])
            block.tensor(_)
            for qc in range(8):
                K = 4 * (qc + 1)
                G = K // 2
                avb = self.p_bc if qc % 2 == 0 else self.p_bc2
                avnm = f"p3av{qc % 2}"
                exp_done = {}
                mask_done = {}
                avsum_done = {}
                LAG = 2
                for g in range(G + LAG):
                    if pending and g == 2:
                        pending.pop(0)()
                    if pending and g == 3:
                        pending.pop(0)()
                    def _(t):
                        if g < G:
                            buf = g % 2
                            if g >= 2:
                                t.wait_ge(self.sA.h, exp_done[g - 2])
                            else:
                                self.guard(t, f"p3s{buf}")
                            ins = None
                            for j in range(2):
                                kt = 2 * g + j
                                ins = t.matmul(self.p_mm[:, buf * 1024 + j * 512: buf * 1024 + (j + 1) * 512],
                                               kt_h[h % 2][:, kt * 128:(kt + 1) * 128],
                                               q_h[h % 2][:, qc * 512:(qc + 1) * 512],
                                               start=True, stop=True)
                            self.sMM.inc(ins)
                        if g >= LAG:
                            gg = g - LAG
                            mc, mn = mask_done[gg]
                            t.wait_ge(mc.h, mn)
                            if gg == 0:
                                self.guard(t, avnm)
                                self.guard(t, "p3st")
                            ins = None
                            for j in range(2):
                                kt = 2 * gg + j
                                a_ap = att[:, (gg % 3) * 1024 + j * 512: (gg % 3) * 1024 + (j + 1) * 512]
                                t.matmul(avb[:],
                                         v_h[h % 2][:, kt * 128:(kt + 1) * 128],
                                         a_ap,
                                         start=(kt == 0), stop=(kt == K - 1))
                                ins = t.matmul(self.p_st1[:], self.o128b[:], a_ap,
                                               start=(kt == 0), stop=(kt == K - 1))
                            self.sMM.inc(ins)
                            avsum_done[gg] = self.sMM.n
                    block.tensor(_)
                    if g < G:
                        mm_s = self.sMM.n
                        def _(s):
                            s.wait_ge(self.sMM.h, mm_s)
                            if g >= 3:
                                s.wait_ge(self.sMM.h, avsum_done[g - 3])
                                mc, mn = mask_done[g - 3]
                                if mc is self.sV:
                                    s.wait_ge(mc.h, mn)
                            self.sA.inc(s.activation(att[:, (g % 3) * 1024:(g % 3 + 1) * 1024],
                                                     self.p_mm[:, (g % 2) * 1024:(g % 2 + 1) * 1024],
                                                     AF.Exp, scale=SCALE))
                            exp_done[g] = self.sA.n
                            self.mark(f"p3s{g % 2}", self.sA)
                        block.scalar(_)
                        if 2 * g + 1 >= 4 * qc:
                            def _(v):
                                v.wait_ge(self.sA.h, exp_done[g])
                                last = None
                                for j in range(2):
                                    kt = 2 * g + j
                                    r = kt - 4 * qc
                                    if 0 <= r < 4:
                                        ap = att[:, (g % 3) * 1024 + j * 512: (g % 3) * 1024 + (j + 1) * 512]
                                        last = v.tensor_tensor(ap, ap, mask[:, r * 512:(r + 1) * 512], ALU.mult)
                                self.sV.inc(last)
                                mask_done[g] = (self.sV, self.sV.n)
                            block.vector(_)
                        else:
                            mask_done[g] = (self.sA, exp_done[g])
                def _(s, sum_mm=avsum_done[G - 1]):
                    s.wait_ge(self.sMM.h, sum_mm)
                    self.guard(s, "p3rcb")
                    s.activation(rc_sb[:], self.p_st1[:], AF.Ln)
                    self.sA.inc(s.activation(rc_b[:], rc_sb[:], AF.Exp, scale=-1.0))
                    self.mark("p3st", self.sA)
                block.scalar(_)
                rc_done_n = self.sA.n
                def mk_tail(h=h, qc=qc, avb=avb, avnm=avnm, rc_done=rc_done_n):
                    def tail_a():
                        def _(t):
                            t.wait_ge(self.sA.h, rc_done)
                            self.guard(t, "p3bcD")
                            self.sMM.inc(t.matmul(self.p_d[:], self.ones1b[:], rc_b[:],
                                                  start=True, stop=True))
                            self.mark("p3rcb", self.sMM)
                        block.tensor(_)
                        self._p3bc = self.sMM.n
                    def tail_b():
                        bc_mm = self._p3bc
                        def _(v):
                            v.wait_ge(self.sMM.h, bc_mm)
                            v.tensor_copy(rb_sb[:], self.p_d[:])
                            self.guard(v, f"p3o{qc % 2}")
                            self.sV.inc(v.tensor_tensor(o_sb[qc % 2][:], avb[:], rb_sb[:], ALU.mult))
                            self.mark("p3bcD", self.sV)
                            self.mark(avnm, self.sV)
                        block.vector(_)
                        o_done = self.sV.n
                        def _(y):
                            y.wait_ge(self.sV.h, o_done)
                            self.sS.inc(y.dma_start(out=self.ao_cc[h, :, qc * 512:(qc + 1) * 512],
                                                    in_=o_sb[qc % 2][:]), 16)
                            self.mark(f"p3o{qc % 2}", self.sS)
                        block.sync(_)
                    return [tail_a, tail_b]
                pending.extend(mk_tail())
            self.mark(f"p3h{h % 2}", self.sMM)
            if do_cc and h % 4 == 3:
                while pending:
                    pending.pop(0)()
                k = h // 4
                st_now = self.sS.n
                def _(g):
                    g.wait_ge(self.sS.h, st_now)
                    self.sCC.inc(g.collective_compute(
                        "AllGather", ALU.bypass,
                        replica_groups=[[0, 1], [2, 3], [4, 5], [6, 7]],
                        ins=[self.ao_cc[4 * k:4 * (k + 1)].rearrange("h p m -> (h p) m")],
                        outs=[self.ao_g[k].rearrange("h p m -> (h p) m")]))
                    if not hasattr(self, "p3_cc"):
                        self.p3_cc = []
                    self.p3_cc.append(self.sCC.n)
                block.gpsimd(_)
        while pending:
            pending.pop(0)()
        self.phase_close(block)

    def p4_wo(self, block, cc_counts=None):
        self.phase_open()
        sb = self.psb
        at_sb = sb("p4_at", [128, NT * HALF], BF16)
        cnd = [sb(f"p4_cnd{i}", [128, 8 * 512], BF16) for i in range(2)]
        tmpb = sb("p4_tmp", [128, 8 * 512], BF16)
        wos = [sb(f"p4_wo{i}", [128, NT * 128], BF16) for i in range(2)]
        yo = [sb(f"p4_yo{i}", [128, 2048], BF16) for i in range(2)]
        for k in range(4):
            for c in range(4):
                def _(y):
                    if cc_counts is not None:
                        y.wait_ge(self.sCC.h, cc_counts[k])
                    self.guard(y, "p4cnd")
                    self.sS.inc(y.dma_start(
                        out=cnd[0][:].rearrange("p (i m) -> p i m", i=8),
                        in_=self.ao_g[k][:, :, c * 512:(c + 1) * 512].rearrange(
                            "i p m -> p i m")), 16)
                    self.sS.inc(y.dma_start(
                        out=cnd[1][:].rearrange("p (i m) -> p i m", i=8),
                        in_=self.ao_g[k][:, :, HALF + c * 512: HALF + (c + 1) * 512].rearrange(
                            "i p m -> p i m")), 16)
                block.sync(_)
                c_ld = self.sS.n
                def _(v):
                    v.wait_ge(self.sS.h, c_ld)
                    v.tensor_scalar(tmpb[:], cnd[0][:], self.sel_sb[:, 0:1], None, ALU.mult)
                    at3 = at_sb[:].rearrange("p (dt m) -> p dt m", dt=NT)
                    ins = None
                    for s_ in range(2):
                        dst = at3[:, 16 * s_ + 4 * k: 16 * s_ + 4 * k + 4,
                                  c * 512:(c + 1) * 512]
                        src1 = cnd[1][:].rearrange("p (i m) -> p i m", i=8)[:, 4 * s_:4 * s_ + 4]
                        srct = tmpb[:].rearrange("p (i m) -> p i m", i=8)[:, 4 * s_:4 * s_ + 4]
                        ins = v.scalar_tensor_tensor(
                            dst, src1, self.sel_sb[:, 1:2], srct, ALU.mult, ALU.add)
                    self.sV.inc(ins)
                    self.mark("p4cnd", self.sV)
                block.vector(_)
        blend_done = self.sV.n
        def _(t):
            t.wait_ge(self.sV.h, blend_done)
        block.tensor(_)
        for ot in range(32):
            def _(g):
                self.guard(g, f"p4w{ot % 2}")
                self.sG.inc(g.dma_start(out=wos[ot % 2][:], in_=self.wo_img[ot]), 16)
            block.gpsimd(_)
            w_ld = self.sG.n
            banks = []
            def _(t):
                t.wait_ge(self.sG.h, w_ld)
                for ch in range(4):
                    b = self.mm_bank(t)
                    banks.append(b)
                    ins = None
                    for dt in range(NT):
                        ins = t.matmul(self.p_mm[:, b * 512:(b + 1) * 512],
                                       wos[ot % 2][:, dt * 128:(dt + 1) * 128],
                                       at_sb[:, dt * HALF + ch * 512: dt * HALF + (ch + 1) * 512],
                                       start=(dt == 0), stop=(dt == NT - 1))
                    self.sMM.inc(ins)
            block.tensor(_)
            self.mark(f"p4w{ot % 2}", self.sMM)
            mm_base = self.sMM.n - 4
            def _(s):
                self.guard(s, f"p4y{ot % 2}")
                for ch in range(4):
                    s.wait_ge(self.sMM.h, mm_base + ch + 1)
                    b = banks[ch]
                    self.sA.inc(s.activation(
                        yo[ot % 2][:, ch * 512:(ch + 1) * 512],
                        self.p_mm[:, b * 512:(b + 1) * 512],
                        AF.Identity, bias=self.bo_sb[:, ot:ot + 1]))
                    self.mark(f"bk{b}", self.sA)
            block.scalar(_)
            ev = self.sA.n
            def _(y):
                y.wait_ge(self.sA.h, ev)
                self.sS.inc(y.dma_start(out=self.y_s[:, ot, :], in_=yo[ot % 2][:]), 16)
            block.sync(_)
            self.mark(f"p4y{ot % 2}", self.sS)
        self.phase_close(block)

    def p56_ffn(self, block):
        self.p5_ln23(block)
        self.p6_ffn(block)

    def p5_ln23(self, block):
        self.phase_open()
        sb = self.psb
        rt_all = self.rt_all
        ics_sb = sb("p5_ics", [8, D], BF16)
        yc = sb("p5_yc", [128, NT * 512], BF16)
        xc = sb("p5_xc", [128, NT * 512], BF16)
        xs_c = sb("p5_xsc", [128, NT * 512], BF16)
        e8 = sb("p5_e8", [8, 512], F32)
        w = self.ln_scratch("p5")
        tln = w.sq
        xf = yc
        def _(g):
            self.sG.inc(g.dma_start(out=ics_sb[:], in_=self.ics_i), 16)
        block.gpsimd(_)
        cs_ld = self.sG.n
        for c in range(4):
            def _(y):
                self.guard(y, "p5yc")
                self.sS.inc(y.dma_start(
                    out=yc[:].rearrange("p (t m) -> p t m", t=NT),
                    in_=self.y_s[:, :, c * 512:(c + 1) * 512]), 16)
            block.sync(_)
            y_ld = self.sS.n
            def _(g):
                self.guard(g, "p5xc")
                self.sG.inc(g.dma_start(
                    out=xc[:].rearrange("p (t m) -> p t m", t=NT),
                    in_=self.x_own[:, :, c * 512:(c + 1) * 512]), 16)
            block.gpsimd(_)
            x_ld = self.sG.n
            def _(v):
                v.wait_ge(self.sS.h, y_ld)
            block.vector(_)
            bc1 = self.ln_stats(block, yc, w, waits=[(self.sS, y_ld)])
            def _(v):
                v.wait_ge(self.sMM.h, bc1)
                v.wait_ge(self.sG.h, x_ld)
                self.sV.inc(self.ln_norm(v, yc[:], yc, tln, res_ap=xc[:]))
                self.mark("p_bc", self.sV)
                self.mark("p5xc", self.sV)
            block.vector(_)
            x2_done = self.sV.n
            def _(y):
                y.wait_ge(self.sV.h, x2_done)
                self.sS.inc(y.dma_start(
                    out=self.x2_s[:, :, c * 512:(c + 1) * 512],
                    in_=yc[:].rearrange("p (t m) -> p t m", t=NT)), 16)
            block.sync(_)
            bc2 = self.ln_stats(block, yc, w, waits=[])
            def _(v):
                v.wait_ge(self.sMM.h, bc2)
                self.sV.inc(self.ln_norm(v, xf[:], yc, tln))
                self.mark("p_bc", self.sV)
            block.vector(_)
            xf_done = self.sV.n
            def _(t):
                t.wait_ge(self.sV.h, xf_done)
                t.wait_ge(self.sG.h, cs_ld)
                self.guard(t, "p5lg")
                ins = None
                for dt in range(NT):
                    ins = t.matmul(self.p_bc2[0:8, :], self.gw_sb[:, dt * 8:(dt + 1) * 8],
                                   xf[:, dt * 512:(dt + 1) * 512],
                                   start=(dt == 0), stop=(dt == NT - 1))
                self.sMM.inc(ins)
            block.tensor(_)
            lg_mm = self.sMM.n
            def _(s):
                s.wait_ge(self.sMM.h, lg_mm)
                self.guard(s, "p5e8")
                self.sA.inc(s.activation(e8[:], self.p_bc2[0:8, :], AF.Exp))
                self.mark("p5lg", self.sA)
            block.scalar(_)
            e8_done = self.sA.n
            def _(t):
                t.wait_ge(self.sA.h, e8_done)
                self.guard(t, "p5st")
                self.sMM.inc(t.matmul(self.p_st1[:], self.o128f[0:8, :], e8[:],
                                      start=True, stop=True))
            block.tensor(_)
            s8_mm = self.sMM.n
            def _(v):
                v.wait_ge(self.sMM.h, s8_mm)
                self.sV.inc(v.reciprocal(w.r[:], self.p_st1[:]))
                self.mark("p5st", self.sV)
            block.vector(_)
            rc8 = self.sV.n
            def _(t):
                t.wait_ge(self.sV.h, rc8)
                self.guard(t, "p5rb")
                self.sMM.inc(t.matmul(self.p_bc[0:8, :], self.ones1f[:, 0:8], w.r[:],
                                      start=True, stop=True))
            block.tensor(_)
            rb8 = self.sMM.n
            def _(v):
                v.wait_ge(self.sMM.h, rb8)
                self.sV.inc(v.tensor_tensor(rt_all[:, c * 512:(c + 1) * 512],
                                            e8[:], self.p_bc[0:8, :], ALU.mult))
                self.mark("p5rb", self.sV)
                self.mark("p5e8", self.sV)
            block.vector(_)
            rt_done = self.sV.n
            for dt in range(NT):
                def _(t):
                    if dt == 0:
                        t.wait_ge(self.sV.h, rt_done)
                        self.guard(t, "p5xsc")
                    b = self.mm_bank(t)
                    self.sMM.inc(t.matmul(self.p_mm[:, b * 512:(b + 1) * 512],
                                          ics_sb[:, dt * 128:(dt + 1) * 128],
                                          rt_all[0:8, c * 512:(c + 1) * 512],
                                          start=True, stop=True))
                    self._p5b = b
                block.tensor(_)
                is_mm = self.sMM.n
                def _(v):
                    v.wait_ge(self.sMM.h, is_mm)
                    b = self._p5b
                    self.sV.inc(v.tensor_tensor(
                        xs_c[:, dt * 512:(dt + 1) * 512],
                        xf[:, dt * 512:(dt + 1) * 512],
                        self.p_mm[:, b * 512:(b + 1) * 512], ALU.mult))
                    self.mark(f"bk{b}", self.sV)
                block.vector(_)
            xs_done = self.sV.n
            self.mark("p5yc", self.sV)
            self.mark("p5yc", self.sS, add=True)
            def _(y):
                y.wait_ge(self.sV.h, xs_done)
                self.sS.inc(y.dma_start(
                    out=self.xs_s[:, :, c * 512:(c + 1) * 512],
                    in_=xs_c[:].rearrange("p (t m) -> p t m", t=NT)), 16)
            block.sync(_)
            self.mark("p5xsc", self.sS)
        self.phase_close(block)

    def p6_ffn(self, block):
        self.phase_open()
        sb = self.psb
        xs_sb = sb("p6_xs", [128, NT * HALF], BF16)
        ocs_sb = sb("p6_ocs", [8, D], BF16)
        wfb = [sb(f"p6_wf{i}", [128, NT * 128], BF16) for i in range(2)]
        fo = [sb(f"p6_fo{i}", [128, 2048], BF16) for i in range(2)]
        ftmp = sb("p6_ftmp", [128, 512], BF16)
        rt_all = self.rt_all
        def _(g):
            self.sG.inc(g.dma_start(out=ocs_sb[:], in_=self.ocs_i), 16)
        block.gpsimd(_)
        def _(y):
            self.sS.inc(y.dma_start(
                out=xs_sb[:].rearrange("p (t m) -> p t m", t=NT),
                in_=self.xs_s[:, :, :]), 16)
        block.sync(_)
        xs_ld = self.sS.n
        oc_ld = self.sG.n
        for ot in range(32):
            def _(g):
                self.guard(g, f"p6sg{ot % 2}")
                self.sG.inc(g.dma_start(out=wfb[ot % 2][:], in_=self.wf_img[ot]), 16)
            block.gpsimd(_)
            w_ld = self.sG.n
            def _(s):
                s.wait_ge(self.sG.h, w_ld)
                self.sA.inc(s.activation(wfb[ot % 2][:], wfb[ot % 2][:], AF.Sign))
            block.scalar(_)
            sg_done = self.sA.n
            for c in range(4):
                banks = []
                def _(t):
                    if c == 0:
                        t.wait_ge(self.sA.h, sg_done)
                        if ot == 0:
                            t.wait_ge(self.sS.h, xs_ld)
                            t.wait_ge(self.sG.h, oc_ld)
                    b = self.mm_bank(t)
                    banks.append(b)
                    ins = None
                    for dt in range(NT):
                        ins = t.matmul(self.p_mm[:, b * 512:(b + 1) * 512],
                                       wfb[ot % 2][:, dt * 128:(dt + 1) * 128],
                                       xs_sb[:, dt * HALF + c * 512: dt * HALF + (c + 1) * 512],
                                       start=(dt == 0), stop=(dt == NT - 1))
                    self.sMM.inc(ins)
                    if c == 3:
                        self.mark(f"p6sg{ot % 2}", self.sMM)
                    self.guard(t, "p6os")
                    self.sMM.inc(t.matmul(self.p_bc[:], ocs_sb[:, ot * 128:(ot + 1) * 128],
                                          rt_all[0:8, c * 512:(c + 1) * 512],
                                          start=True, stop=True))
                block.tensor(_)
                mm_c = self.sMM.n
                def _(s):
                    s.wait_ge(self.sMM.h, mm_c - 1)
                    self.guard(s, "p6t")
                    b = banks[0]
                    self.sA.inc(s.activation(ftmp[:], self.p_mm[:, b * 512:(b + 1) * 512],
                                             AF.Identity, bias=self.bf_sb[:, ot:ot + 1]))
                    self.mark(f"bk{b}", self.sA)
                block.scalar(_)
                ft_done = self.sA.n
                def _(v):
                    v.wait_ge(self.sA.h, ft_done)
                    v.wait_ge(self.sMM.h, mm_c)
                    self.guard(v, f"p6fo{ot % 2}")
                    self.sV.inc(v.tensor_tensor(fo[ot % 2][:, c * 512:(c + 1) * 512],
                                                ftmp[:], self.p_bc[:], ALU.mult))
                    self.mark("p6os", self.sV)
                    self.mark("p6t", self.sV)
                block.vector(_)
            fo_done = self.sV.n
            def _(y):
                y.wait_ge(self.sV.h, fo_done)
                self.sS.inc(y.dma_start(out=self.fo_s[:, ot, :], in_=fo[ot % 2][:]), 16)
            block.sync(_)
            self.mark(f"p6fo{ot % 2}", self.sS)
        self.phase_close(block)

    def p7_out(self, block):
        self.phase_open()
        foc = self.psb("p7_fo", [128, NT * 512], BF16)
        x2c = self.psb("p7_x2", [128, NT * 512], BF16)
        outc = self.psb("p7_out", [128, NT * 512], F32)
        w = self.ln_scratch("p7")
        tln = w.sq
        for c in range(4):
            def _(y):
                self.guard(y, "p7f")
                self.sS.inc(y.dma_start(
                    out=foc[:].rearrange("p (t m) -> p t m", t=NT),
                    in_=self.fo_s[:, :, c * 512:(c + 1) * 512]), 16)
                self.sS.inc(y.dma_start(
                    out=x2c[:].rearrange("p (t m) -> p t m", t=NT),
                    in_=self.x2_s[:, :, c * 512:(c + 1) * 512]), 16)
            block.sync(_)
            ld = self.sS.n
            bc = self.ln_stats(block, foc, w, waits=[(self.sS, ld)])
            def _(v):
                v.wait_ge(self.sMM.h, bc)
                v.wait_ge(self.sS.h, ld)
                self.guard(v, "p7o")
                self.sV.inc(self.ln_norm(v, outc[:], foc, tln, res_ap=x2c[:]))
                self.mark("p_bc", self.sV)
                self.mark("p7f", self.sV)
            block.vector(_)
            o_done = self.sV.n
            def _(y):
                y.wait_ge(self.sV.h, o_done)
                self.sS.inc(y.dma_start(
                    out=self.out_img[:, :, c * 512:(c + 1) * 512],
                    in_=outc[:].rearrange("p (t m) -> p t m", t=NT)), 16)
            block.sync(_)
            self.mark("p7o", self.sS)
        self.phase_close(block)

    def finalize(self):
        self.ctx.close()
        return self.nc


def build(phases=("p0", "p1", "p2"), dbg_modes=None):
    mk = MK(dbg_modes)
    mk.declare()
    with mk.nc.Block() as block:
        if "p0" in phases:
            mk.p0_consts(block)
        if "p1" in phases:
            mk.p1_ln1(block)
        if "p2" in phases:
            mk.p2_qkv(block)
        if "p3" in phases:
            mk.p3_att(block, do_cc=("cc" in phases))
        if "p4" in phases:
            mk.p4_wo(block, cc_counts=mk.p3_cc if ("cc" in phases) else None)
        if "p56" in phases:
            mk.p56_ffn(block)
        if "p7" in phases:
            mk.p7_out(block)
        def _(y):
            y.wait_ge(mk.sS.h, mk.sS.n)
            y.wait_ge(mk.sG.h, mk.sG.n)
        block.sync(_)
    return mk


def host_inputs(core, x, wq, bq, wk, bk, wv, bv, wo, bo, wf, bf, gw, ics, ocs):
    b, hh = core // 2, core % 2
    sl = slice(hh * 2048, (hh + 1) * 2048)
    xT = np.ascontiguousarray(x[b].T)
    d = {}
    d["x_img"] = np.ascontiguousarray(xT.reshape(NT, 128, L).transpose(1, 0, 2))
    d["x_own"] = np.ascontiguousarray(xT[:, sl].reshape(NT, 128, HALF).transpose(1, 0, 2))
    def wimg_qk(w):
        wt = np.ascontiguousarray(w.T[:, sl])
        return np.ascontiguousarray(
            wt.reshape(NT, 128, NH, 128).transpose(2, 1, 0, 3).reshape(NH, 128, NT * 128))
    d["wq_img"] = wimg_qk(wq)
    d["wk_img"] = wimg_qk(wk)
    wvt = np.ascontiguousarray(wv.T[:, sl])
    d["wv_img"] = np.ascontiguousarray(
        wvt.reshape(NT, 128, 8, 256).transpose(2, 1, 0, 3).reshape(8, 128, NT * 256))
    def wimg_full(w):
        wt = np.ascontiguousarray(w.T)
        return np.ascontiguousarray(
            wt.reshape(NT, 128, 32, 128).transpose(2, 1, 0, 3).reshape(32, 128, NT * 128))
    d["wo_img"] = wimg_full(wo)
    d["wf_img"] = wimg_full(wf)
    d["bq_i"] = np.ascontiguousarray(bq[sl].reshape(NH, 128).T)
    d["bk_i"] = np.ascontiguousarray(bk[sl].reshape(NH, 128).T)
    d["bv_i"] = np.ascontiguousarray(bv[sl].reshape(1, NH * 128))
    d["bo_i"] = np.ascontiguousarray(bo.reshape(32, 128).T)
    d["bf_i"] = np.ascontiguousarray(bf.reshape(32, 128).T)
    d["gw_img"] = np.ascontiguousarray(gw.T.reshape(NT, 128, 8).transpose(1, 0, 2))
    d["ics_i"] = np.ascontiguousarray(ics)
    d["ocs_i"] = np.ascontiguousarray(ocs)
    k = np.arange(128)[:, None]
    j = np.arange(512)[None, :]
    masks = np.stack([(j >= k + 128 * r).astype(np.float32) for r in range(4)], axis=1)
    d["mask_i"] = np.ascontiguousarray(masks)
    sel = np.zeros((128, 2), np.float32)
    sel[:, 0] = 1.0 - hh
    sel[:, 1] = hh
    d["sel_i"] = sel
    return d


from concourse.bass_utils import run_bass_kernel_spmd


def kernel(L0_attn_x, L0_attn_Wq, L0_attn_bq, L0_attn_Wk, L0_attn_bk,
           L0_attn_Wv, L0_attn_bv, L0_attn_Wo, L0_attn_bo,
           L0_ffn_weight, L0_ffn_bias, L0_ffn_gate_w,
           L0_ffn_in_channel_scale, L0_ffn_out_channel_scale):
    x = np.asarray(L0_attn_x, dtype=np.float32)
    args = [np.asarray(a, dtype=np.float32) for a in
            (L0_attn_Wq, L0_attn_bq, L0_attn_Wk, L0_attn_bk, L0_attn_Wv,
             L0_attn_bv, L0_attn_Wo, L0_attn_bo, L0_ffn_weight, L0_ffn_bias,
             L0_ffn_gate_w, L0_ffn_in_channel_scale, L0_ffn_out_channel_scale)]
    (wq, bq, wk, bk, wv, bv, wo, bo, wf, bf, gw, ics, ocs) = args

    mk = build(phases=("p0", "p1", "p2", "p3", "cc", "p4", "p56", "p7"))
    nc = mk.finalize()
    in_maps = [host_inputs(c, x, wq, bq, wk, bk, wv, bv, wo, bo, wf, bf, gw, ics, ocs)
               for c in range(8)]
    res = run_bass_kernel_spmd(nc, in_maps, core_ids=list(range(8)))
    out = np.empty((4, L, D), np.float32)
    for c in range(8):
        b, hh = c // 2, c % 2
        img = np.asarray(res.results[c]["out_img"], dtype=np.float32)
        outT = img.reshape(128, NT, HALF).transpose(1, 0, 2).reshape(D, HALF)
        out[b, hh * HALF:(hh + 1) * HALF, :] = outT.T
    return out
